# version 14
# baseline (speedup 1.0000x reference)
import contextlib
import math
import numpy as np
import concourse.bass as bass
import concourse.mybir as mybir
from concourse.bass_utils import run_bass_kernel_spmd

F32 = mybir.dt.float32
BF16 = mybir.dt.bfloat16
I8 = mybir.dt.int8
AF = mybir.ActivationFunctionType
ALU = mybir.AluOpType
AX = mybir.AxisListType

ENGS = ("sync", "gpsimd", "tensor", "vector", "scalar")

ALPHA = 1.0 / math.sqrt(128.0)
M0 = 12.0


class Sched:

    def __init__(self, nc, ctx):
        self.nc = nc
        self.ctx = ctx
        self.ops = {e: [] for e in ENGS}
        self.count = {}
        self.sems = {}
        for e in ("tensor", "vector", "scalar"):
            self._mk(e)

    def _mk(self, name):
        self.sems[name] = self.ctx.enter_context(self.nc.semaphore("s_" + name))
        self.count[name] = 0
        return self.sems[name]

    def emit(self, eng, builder, waits=(), inc=0, sem=None):
        if inc and sem is None:
            sem = eng
        if sem is not None and sem not in self.sems:
            self._mk(sem)
        waits = [(sn, v) for (sn, v) in waits if v > 0]
        n = 0
        if inc:
            self.count[sem] += inc
            n = self.count[sem]
        sched = self

        def op(e):
            for sn, v in waits:
                e.wait_ge(sched.sems[sn], v)
            inst = builder(e)
            if inc:
                inst.then_inc(sched.sems[sem], inc)
        self.ops[eng].append(op)
        return n

    def snap(self):
        return dict(self.count)

    def replay(self, block):
        for eng in ENGS:
            def mk(eng):
                def run(e):
                    for op in self.ops[eng]:
                        op(e)
                return run
            getattr(block, eng)(mk(eng))


def build(T=4096, E=2048, I=8192, H=16):
    D = 128
    EC = E // 128
    IC = I // 128
    THEAD = T // H
    VQ = THEAD * EC
    NQT = VQ // 128
    QW = min(512, VQ)
    NQUART = VQ // QW
    NACC = QW // 128
    G = 2
    NGRP = H // G
    GT = G * THEAD
    TW = GT
    NWT = TW // 128
    NWIN = T // TW
    NTILE = T // 128
    NCH = max(1, E // 512)
    SL = min(16, IC)
    assert NACC <= 4 and GT == TW and EC == H

    nc = bass.Bass("TRN2", target_bir_lowering=False)
    ap = lambda *a, **k: nc.dram_tensor(*a, **k).ap()
    x = ap("x", [T, E], F32, kind="ExternalInput")
    WqT = ap("WqT", [EC, 128, E], F32, kind="ExternalInput")
    WkT = ap("WkT", [EC, 128, E], F32, kind="ExternalInput")
    WvT = ap("WvT", [EC, 128, E], F32, kind="ExternalInput")
    WpT = ap("WpT", [EC, 128, E], F32, kind="ExternalInput")
    bp = ap("bp", [128, EC], F32, kind="ExternalInput")
    w_gate = ap("w_gate", [IC, 128, E], I8, kind="ExternalInput")
    w_up = ap("w_up", [IC, 128, E], I8, kind="ExternalInput")
    w_down = ap("w_down", [EC, IC // SL, 128, SL * 128], I8, kind="ExternalInput")
    s_gate = ap("s_gate", [128, IC], F32, kind="ExternalInput")
    s_up = ap("s_up", [128, IC], F32, kind="ExternalInput")
    s_down = ap("s_down", [128, EC], F32, kind="ExternalInput")
    ident = ap("ident", [128, 128], F32, kind="ExternalInput")
    masku = ap("masku", [128, 128], F32, kind="ExternalInput")
    out = ap("out", [T, E], F32, kind="ExternalOutput")

    xnT_s = ap("xnT_s", [EC, 128, T], BF16, kind="Internal")
    ctx_s = ap("ctx_s", [H, 128, VQ], BF16, kind="Internal")

    with contextlib.ExitStack() as ctx:
        sb = lambda name, shape, dt: ctx.enter_context(nc.sbuf_tensor(name, shape, dt))
        psumt = lambda name, shape, dt: ctx.enter_context(nc.psum_tensor(name, shape, dt))

        o_vext = 0
        o_vhT = o_vext + NQT * D
        o_ctx = o_vhT + 2 * VQ
        o_pT = o_ctx + 2 * VQ
        o_xn0 = o_pT + 2 * QW
        o_xnt = o_xn0 + E
        big3_cols = max(IC * TW, o_xnt + 4 * E)

        id_bf = sb("id_bf", [128, 128], BF16)
        mask_bf = sb("mask_bf", [128, 128], BF16)
        sg_sb = sb("sg_sb", [128, IC], F32)
        su_sb = sb("su_sb", [128, IC], F32)
        sd_sb = sb("sd_sb", [128, EC], F32)
        bp_sb = sb("bp_sb", [128, EC], F32)
        st6 = sb("st6", [128, 6 * NCH], F32)
        mv = sb("mv", [128, 2], F32)
        sdv = sb("sdv", [128, 2], F32)
        rcp = sb("rcp", [128, 2], F32)
        gu_g = sb("gu_g", [128, TW], BF16)
        gu_u = sb("gu_u", [128, TW], BF16)
        r_eoc = sb("r_eoc", [128, 3 * TW], BF16)

        big1 = sb("big1", [128, 2 * EC * GT], BF16)
        big2 = sb("big2", [128, max(2 * H * TW, 4 * VQ)], BF16)
        big3 = sb("big3", [128, big3_cols], BF16)
        big4 = sb("big4", [128, 3 * EC * 128], BF16)
        wgu = sb("wgu", [128, 4 * EC * 128], BF16)
        wds = sb("wds", [128, 2 * SL * 128], BF16)
        acc = sb("acc", [128, QW], F32)
        den_sb = sb("den_sb", [1, QW], BF16)
        scr_bf = sb("scr_bf", [128, QW], BF16)
        xt = sb("xt", [128, NWT * E], F32)

        def xnT_buf(par):
            return big1[:, par * EC * GT:(par + 1) * EC * GT]

        def qh_ap(hh):
            return big2[:, hh * VQ:(hh + 1) * VQ]

        def kh_ap(hh):
            return big2[:, (2 + hh) * VQ:(3 + hh) * VQ]

        vext = big3[:, o_vext:o_vhT]

        def vhT_ap(hh):
            return big3[:, o_vhT + hh * VQ: o_vhT + (hh + 1) * VQ]

        def ctx_sb(par):
            return big3[:, o_ctx + par * VQ: o_ctx + (par + 1) * VQ]

        def pT_ap(par):
            return big3[:, o_pT + par * QW: o_pT + (par + 1) * QW]

        xn0 = big3[:, o_xn0:o_xn0 + E]

        def xnt_dst(a, ec):
            off = o_xnt + ec * 4 * 128 + a * 128
            return big3[:, off:off + 128]

        xnt_grp = big3[:, o_xnt:o_xnt + 4 * E]

        raw2 = big2[:, 0:H * TW]
        gT2 = big2[:, 0:EC * TW]
        ctxT2 = big2[:, H * TW:2 * H * TW]

        def wqkv_slab(par):
            return big4[:, par * EC * 128:(par + 1) * EC * 128]

        def wg_slab(par):
            return wgu[:, par * EC * 128:(par + 1) * EC * 128]

        def wu_slab(par):
            return wgu[:, (2 + par) * EC * 128:(3 + par) * EC * 128]

        def wd_slab(par):
            return wds[:, par * SL * 128:(par + 1) * SL * 128]

        def xt_ap(slot):
            return xt[:, slot * E:(slot + 1) * E]

        def r_ap(par):
            return r_eoc[:, par * TW:(par + 1) * TW]

        A = [psumt(f"A{i}", [128, 512], F32) for i in range(6)]
        TP = [psumt(f"TP{i}", [128, 1024], BF16) for i in range(2)]
        bankv = [0] * 6
        banks = [0] * 6
        tpv = [0, 0]
        tps = [0, 0]

        s = Sched(nc, ctx)
        block = ctx.enter_context(nc.Block())
        SG = "gpsimd"
        SY = "sync"

        def bankw(b):
            return [("vector", bankv[b]), ("scalar", banks[b])]

        def tpw(p):
            return [("vector", tpv[p]), ("scalar", tps[p])]

        for dst, src in ((id_bf, ident), (mask_bf, masku), (sg_sb, s_gate),
                         (su_sb, s_up), (sd_sb, s_down), (bp_sb, bp)):
            cst = s.emit(SG, lambda e, dst=dst, src=src: e.dma_start(out=dst[:], in_=src),
                         inc=16, sem="c")
        CST = cst

        xt_user_v = [0] * NWT
        xt_user_s = [0] * NWT
        ld_x = [0] * NWT
        tp_par = 0
        spx_cnt = 0
        xn0_free_t = 0
        prev_bna = 0
        prev_std = 0
        prev_xnd = 0

        for i in range(NTILE):
            slot = i % NWT
            ld_x[slot] = s.emit(
                SY, lambda e, i=i, slot=slot: e.dma_start(
                    out=xt_ap(slot), in_=x[i * 128:(i + 1) * 128, :]),
                waits=[("vector", xt_user_v[slot]), ("scalar", xt_user_s[slot])],
                inc=16, sem=f"ldx{slot}")
            w_ld = [(f"ldx{slot}", ld_x[slot])]
            bns = 0
            cw = E // NCH
            for c in range(NCH):
                bns = s.emit("vector", lambda e, c=c, cw=cw, slot=slot: e.bn_stats(
                    st6[:, 6 * c:6 * c + 6], xt_ap(slot)[:, cw * c:cw * (c + 1)]),
                    waits=w_ld + ([("vector", prev_bna)] if c == 0 else []), inc=1)
            bna = s.emit("vector", lambda e: e.bn_aggr(mv[:], st6[:]),
                         waits=[("vector", bns), ("scalar", prev_std),
                                ("vector", prev_xnd)], inc=1)
            prev_bna = bna
            std = s.emit("scalar", lambda e: e.activation(
                sdv[:, 0:1], mv[:, 1:2], AF.Sqrt, bias=np.float32(1e-5)),
                waits=[("vector", bna)], inc=1)
            prev_std = std
            rst = s.emit("vector", lambda e: e.reciprocal(sdv[:, 1:2], sdv[:, 0:1]),
                         waits=[("scalar", std)], inc=1)
            xnd = s.emit("vector", lambda e, slot=slot: e.tensor_scalar(
                xn0, xt_ap(slot), mv[:, 0:1], sdv[:, 1:2], ALU.subtract, ALU.mult),
                waits=[("vector", rst), ("tensor", xn0_free_t)], inc=1)
            prev_xnd = xnd
            xt_user_v[slot] = xnd
            xt_user_s[slot] = std
            a = i % 4
            last_evac = 0
            for ec in range(EC):
                p = tp_par
                tp = s.emit("tensor", lambda e, ec=ec, p=p: e.transpose(
                    TP[p][:, 0:128], xn0[:, ec * 128:(ec + 1) * 128], id_bf[:]),
                    waits=[("vector", xnd), ("c", CST)] + tpw(p), inc=1)
                if ec == EC - 1:
                    xn0_free_t = tp
                wsp = [("spx", 16 * (i // 4))] if (a == 0 and ec == 0) else []
                tps[p] = s.emit("scalar", lambda e, ec=ec, p=p, a=a: e.copy(
                    xnt_dst(a, ec), TP[p][:, 0:128]),
                    waits=[("tensor", tp)] + wsp, inc=1)
                last_evac = tps[p]
                tp_par ^= 1
            if a == 3:
                spx_cnt = s.emit(
                    SY, lambda e, i=i: e.dma_start(
                        out=xnT_s[:, :, (i - 3) * 128:(i + 1) * 128].rearrange(
                            "ec p t -> p ec t"),
                        in_=xnt_grp.rearrange("p (ec t) -> p ec t", t=4 * 128)),
                    waits=[("scalar", last_evac)], inc=16, sem="spx")

        def grp_ready(g):
            return [("spx", 16 * (g + 1))]

        big1_free = [0, 0]
        ldg = [0, 0]
        w1 = [0, 0, 0]
        w1_free = [0, 0, 0]
        qkv_par = 0
        big2_free = 0
        vext_free = 0
        vhT_free = [0, 0]
        ctx_busy = [0, 0]
        ctx_done_all = []
        pT_free = [0, 0]
        head_par = 0
        slab_par = 0
        quart_par = 0
        ev_last = 0
        scr_free = 0
        den_free = 0
        acc_last = 0
        last_ctx_mm = 0
        pT_free_v = [0, 0]
        qkv_ev = [0, 0]

        for g in range(NGRP):
            gpar = g % 2
            ldg[gpar] = s.emit(
                SY, lambda e, g=g, gp=gpar: e.dma_start(
                    out=xnT_buf(gp).rearrange("p (ec t) -> p ec t", ec=EC),
                    in_=xnT_s[:, :, g * GT:(g + 1) * GT].rearrange("ec p t -> p ec t")),
                waits=grp_ready(g) + [("tensor", big1_free[gpar])],
                inc=16, sem=f"ldg{gpar}")

            b2f = big2_free
            mm = 0
            vdone = None
            for WT, proj in ((WvT, "v"), (WqT, "q"), (WkT, "k")):
                for cp in range(EC):
                    sp = slab_par
                    w1[sp] = s.emit(
                        SG, lambda e, WT=WT, cp=cp, sp=sp: e.dma_start(
                            out=wqkv_slab(sp), in_=WT[cp]),
                        waits=[("tensor", w1_free[sp])],
                        inc=16, sem=f"w1{sp}")
                    qp = qkv_par
                    for ec in range(EC):
                        wt = []
                        if ec == 0:
                            wt = [(f"w1{sp}", w1[sp]), (f"ldg{gpar}", ldg[gpar])] + bankw(qp)
                        mm = s.emit("tensor", lambda e, ec=ec, sp=sp, qp=qp, gp=gpar: e.matmul(
                            A[qp][:, 0:GT], wqkv_slab(sp)[:, ec * 128:(ec + 1) * 128],
                            xnT_buf(gp)[:, ec * GT:(ec + 1) * GT],
                            start=(ec == 0), stop=(ec == EC - 1)),
                            waits=wt, inc=(1 if ec == EC - 1 else 0))
                    w1_free[sp] = mm
                    for hh in range(G):
                        if proj == "q":
                            dst = qh_ap(hh)
                        elif proj == "k":
                            dst = kh_ap(hh)
                        else:
                            dst = vhT_ap(hh)
                        evw = [("tensor", mm), ("tensor", b2f)]
                        if proj == "v":
                            evw.append(("tensor", vhT_free[hh]))
                        bankv[qp] = s.emit("vector", lambda e, dst=dst, cp=cp, qp=qp, hh=hh: e.tensor_copy(
                            dst[:, cp::EC], A[qp][:, hh * THEAD:(hh + 1) * THEAD]),
                            waits=evw, inc=1)
                        qkv_ev[hh] = bankv[qp]
                    qkv_par = (qkv_par + 1) % 4
                    slab_par = (slab_par + 1) % 3
                if proj == "v":
                    vdone = [("vector", qkv_ev[0]), ("vector", qkv_ev[1])]
            big1_free[gpar] = mm
            qkv_all = [("vector", qkv_ev[0]), ("vector", qkv_ev[1])]

            for hh in range(G):
                h = g * G + hh
                hpar = head_par
                last_vx = 0
                for kt in range(NQT):
                    p = tp_par
                    tp = s.emit("tensor", lambda e, kt=kt, p=p, hh=hh: e.transpose(
                        TP[p][:, 0:128], vhT_ap(hh)[:, kt * 128:(kt + 1) * 128], id_bf[:]),
                        waits=(vdone if kt <= 1 else []) + tpw(p)
                        + ([("tensor", vext_free)] if kt <= 1 else []), inc=1)
                    if kt == NQT - 1:
                        vhT_free[hh] = tp
                    tps[p] = s.emit("scalar", lambda e, kt=kt, p=p: e.copy(
                        vext[:, kt * D:(kt + 1) * D], TP[p][:, 0:128]),
                        waits=[("tensor", tp)]
                        + ([("tensor", vext_free)] if kt == 0 else []), inc=1)
                    last_vx = tps[p]
                    tp_par ^= 1

                pending = None

                def fin_quarter(cq, qbase, last_mm, acc_l, hp):
                    nonlocal ev_last, scr_free, den_free
                    accb = s.emit("vector", lambda e: e.tensor_copy(
                        scr_bf[:, 0:QW], acc[:, 0:QW]),
                        waits=[("vector", acc_l), ("tensor", scr_free)], inc=1)
                    dmm1 = s.emit("tensor", lambda e: e.matmul(
                        A[3][0:1, 0:QW], mask_bf[:, 127:128], scr_bf[:, 0:QW],
                        start=True, stop=True),
                        waits=[("vector", accb)] + bankw(3), inc=1)
                    scr_free = dmm1
                    dcp = s.emit("vector", lambda e: e.tensor_copy(
                        den_sb[0:1, 0:QW], A[3][0:1, 0:QW]),
                        waits=[("tensor", dmm1), ("tensor", den_free)], inc=1)
                    bankv[3] = dcp
                    bmm = s.emit("tensor", lambda e: e.matmul(
                        A[2][:, 0:QW], mask_bf[0:1, 0:128], den_sb[0:1, 0:QW],
                        start=True, stop=True),
                        waits=[("vector", dcp)] + bankw(2), inc=1)
                    den_free = bmm
                    rbv = s.emit("vector", lambda e: e.reciprocal(
                        acc[:, 0:QW], A[2][:, 0:QW]),
                        waits=[("tensor", bmm)], inc=1)
                    bankv[2] = rbv
                    ev = s.emit("vector", lambda e, cq=cq, hp=hp, qbase=qbase: e.tensor_tensor(
                        ctx_sb(hp)[:, qbase:qbase + QW], A[cq][:, 0:QW],
                        acc[:, 0:QW], ALU.mult),
                        waits=[("tensor", last_mm), ("vector", rbv),
                               (f"sctx{hp}", ctx_busy[hp])], inc=1)
                    bankv[cq] = ev
                    ev_last = ev

                for Qq in range(NQUART):
                    qbase = Qq * QW
                    ktmax = (qbase + QW) // 128
                    cq = quart_par
                    quart_par ^= 1
                    sinfo = {}

                    def emit_S(kt, Qq=Qq, qbase=qbase, hh=hh):
                        qlo = max(qbase, kt * 128)
                        width = qbase + QW - qlo
                        sp = kt % 2
                        pp = kt % 2
                        smm = s.emit("tensor", lambda e, kt=kt, qlo=qlo, width=width, sp=sp, hh=hh: e.matmul(
                            A[4 + sp][:, 0:width], kh_ap(hh)[:, kt * 128:(kt + 1) * 128],
                            qh_ap(hh)[:, qlo:qlo + width], start=True, stop=True),
                            waits=qkv_all + bankw(4 + sp) if kt <= 1 else bankw(4 + sp),
                            inc=1)
                        hi_n = 1
                        mid = width - hi_n * 128
                        exp_hi = s.emit("scalar", lambda e, width=width, mid=mid, sp=sp, pp=pp: e.activation(
                            pT_ap(pp)[:, mid:width], A[4 + sp][:, mid:width], AF.Exp,
                            bias=np.float32(-M0), scale=np.float32(ALPHA)),
                            waits=[("tensor", smm), ("tensor", pT_free[pp]),
                                   ("vector", pT_free_v[pp])],
                            inc=1)
                        exp_lo = exp_hi
                        if mid > 0:
                            exp_lo = s.emit("scalar", lambda e, mid=mid, sp=sp, pp=pp: e.activation(
                                pT_ap(pp)[:, 0:mid], A[4 + sp][:, 0:mid], AF.Exp,
                                bias=np.float32(-M0), scale=np.float32(ALPHA)),
                                inc=1)
                        banks[4 + sp] = exp_lo
                        mskd = 0
                        if qlo == kt * 128:
                            mskd = s.emit("vector", lambda e, pp=pp: e.tensor_tensor(
                                pT_ap(pp)[:, 0:128], pT_ap(pp)[:, 0:128], mask_bf[:], ALU.mult),
                                waits=[("scalar", exp_lo), ("c", CST)], inc=1)
                        sinfo[kt] = (exp_hi, exp_lo, mskd, qlo, hi_n)

                    emit_S(0)
                    for kt in range(ktmax):
                        if kt + 1 < ktmax:
                            emit_S(kt + 1)
                        exp_hi, exp_lo, mskd, qlo, hi_n = sinfo.pop(kt)
                        pp = kt % 2
                        first_qt = max(kt, qbase // 128)
                        qts = list(range(first_qt, qbase // 128 + NACC))[::-1]
                        for qi_, qt in enumerate(qts):
                            j = qt - qbase // 128
                            wt = []
                            if qi_ == 0:
                                wt = [("scalar", exp_hi), ("scalar", last_vx)]
                            if qi_ == hi_n:
                                wt += [("scalar", exp_lo)]
                            if mskd and qi_ == len(qts) - 1:
                                wt.append(("vector", mskd))
                            if kt == 0 and qi_ == 0:
                                wt += bankw(cq)
                            cmm = s.emit("tensor", lambda e, qt=qt, kt=kt, j=j, pp=pp, qlo=qlo, cq=cq, st=(kt == 0 and qi_ == 0), sp_=(kt == ktmax - 1 and qi_ == len(qts) - 1): e.matmul(
                                A[cq][:, j * 128:(j + 1) * 128],
                                vext[:, kt * D:(kt + 1) * D],
                                pT_ap(pp)[:, qt * 128 - qlo:qt * 128 - qlo + 128],
                                start=st, stop=sp_),
                                waits=wt, inc=(1 if qt == qts[-1] else 0))
                            if qt == qts[-1]:
                                last_ctx_mm = cmm
                        pT_free[pp] = last_ctx_mm
                        lo = qlo - qbase
                        aw = [("scalar", exp_lo)]
                        if mskd:
                            aw.append(("vector", mskd))
                        if kt == 0:
                            if pending is not None:
                                fin_quarter(*pending)
                                pending = None
                            acc_last = s.emit("vector", lambda e, pp=pp: e.tensor_copy(
                                acc[:, 0:QW], pT_ap(pp)[:, 0:QW]),
                                waits=aw + [("vector", ev_last)], inc=1)
                        else:
                            acc_last = s.emit("vector", lambda e, pp=pp, lo=lo: e.tensor_tensor(
                                acc[:, lo:QW], acc[:, lo:QW], pT_ap(pp)[:, 0:QW - lo], ALU.add),
                                waits=aw + [("vector", acc_last)], inc=1)
                        pT_free_v[pp] = acc_last
                    pending = (cq, qbase, last_ctx_mm, acc_last, hpar)
                fin_quarter(*pending)
                pending = None
                vext_free = last_ctx_mm
                big2_free = last_ctx_mm
                ctx_busy[hpar] = s.emit(
                    SY, lambda e, h=h, hp=hpar: e.dma_start(
                        out=ctx_s[h], in_=ctx_sb(hp)),
                    waits=[("vector", ev_last)], inc=16, sem=f"sctx{hpar}")
                ctx_done_all = [("sctx0", ctx_busy[0]), ("sctx1", ctx_busy[1])]
                head_par ^= 1

        F = s.snap()
        fence = [("tensor", F["tensor"]), ("vector", F["vector"]), ("scalar", F["scalar"])]

        ld2 = [0, 0]
        raw_free = 0
        gT_user = 0
        ctxT_user = 0
        wgp = [0, 0]
        wgp_free = [0, 0]
        wup = [0, 0]
        wup_free = [0, 0]
        wdp = [0, 0]
        wdp_free = [0, 0]
        wpp = list(w1)
        wpp_free = list(w1_free)
        st_out = [0] * NWT
        gate_par = 0
        dpar = 0
        fuse_user = 0
        gsb_free = 0
        r_free = [0, 0, 0]
        first2 = True

        for w in range(NWIN):
            wpar = w % 2
            wt0 = list(fence) if first2 else []
            ld2[wpar] = s.emit(
                SY, lambda e, w=w, wp=wpar: e.dma_start(
                    out=xnT_buf(wp).rearrange("p (ec t) -> p ec t", ec=EC),
                    in_=xnT_s[:, :, w * TW:(w + 1) * TW].rearrange("ec p t -> p ec t")),
                waits=wt0 + grp_ready(w) + [("tensor", big1_free[wpar])],
                inc=16, sem=f"ldg{wpar}")
            ldc = s.emit(
                SY, lambda e, w=w: e.dma_start(
                    out=ctxT2.rearrange("p (h t) -> p h t", h=H),
                    in_=ctx_s[:, :, w * TW:(w + 1) * TW].rearrange("h p t -> p h t")),
                waits=wt0 + ctx_done_all + [("tensor", ctxT_user)],
                inc=16, sem="ldc")
            for i in range(NWT):
                ld_x[i] = s.emit(
                    SY, lambda e, w=w, i=i: e.dma_start(
                        out=xt_ap(i), in_=x[(w * NWT + i) * 128:(w * NWT + i + 1) * 128, :]),
                    waits=wt0 + [(f"sout{i}", st_out[i]),
                                 ("vector", xt_user_v[i]), ("scalar", xt_user_s[i])],
                    inc=16, sem=f"ldx{i}")
                xt_user_v[i] = 0
                xt_user_s[i] = 0
            first2 = False

            def emit_gu(ic):
                sp = ic % 2
                wgp[sp] = s.emit(
                    SG, lambda e, ic=ic, sp=sp: e.dma_start(
                        out=wg_slab(sp), in_=w_gate[ic]),
                    waits=[("tensor", wgp_free[sp])],
                    inc=16, sem=f"wg{sp}")
                wup[sp] = s.emit(
                    SG, lambda e, ic=ic, sp=sp: e.dma_start(
                        out=wu_slab(sp), in_=w_up[ic]),
                    waits=[("tensor", wup_free[sp])],
                    inc=16, sem=f"wu{sp}")

            for ic in range(IC):
                sp = ic % 2
                if w == 0 or ic >= 2:
                    emit_gu(ic)
                gp = gate_par
                mmg = 0
                for ec in range(EC):
                    wt = []
                    if ec == 0:
                        wt = [(f"wg{sp}", wgp[sp]), (f"ldg{wpar}", ld2[wpar])] + bankw(gp)
                    mmg = s.emit("tensor", lambda e, ec=ec, sp=sp, gp=gp, wp=wpar: e.matmul(
                        A[gp][:, 0:TW], wg_slab(sp)[:, ec * 128:(ec + 1) * 128],
                        xnT_buf(wp)[:, ec * TW:(ec + 1) * TW],
                        start=(ec == 0), stop=(ec == EC - 1)),
                        waits=wt, inc=(1 if ec == EC - 1 else 0))
                mmu = 0
                for ec in range(EC):
                    wt = []
                    if ec == 0:
                        wt = [(f"wu{sp}", wup[sp])] + bankw(2 + gp)
                    mmu = s.emit("tensor", lambda e, ec=ec, sp=sp, gp=gp, wp=wpar: e.matmul(
                        A[2 + gp][:, 0:TW], wu_slab(sp)[:, ec * 128:(ec + 1) * 128],
                        xnT_buf(wp)[:, ec * TW:(ec + 1) * TW],
                        start=(ec == 0), stop=(ec == EC - 1)),
                        waits=wt, inc=(1 if ec == EC - 1 else 0))
                wgp_free[sp] = mmg
                wup_free[sp] = mmu
                big1_free[wpar] = mmu
                gev = s.emit("scalar", lambda e, ic=ic, gp=gp: e.activation(
                    gu_g[:], A[gp][:, 0:TW], AF.Silu, scale=sg_sb[:, ic:ic + 1]),
                    waits=[("tensor", mmg), ("vector", gsb_free)], inc=1)
                banks[gp] = gev
                uev = s.emit("vector", lambda e, ic=ic, gp=gp: e.tensor_scalar(
                    gu_u[:], A[2 + gp][:, 0:TW], su_sb[:, ic:ic + 1], None, ALU.mult),
                    waits=[("tensor", mmu), ("vector", gsb_free)], inc=1)
                bankv[2 + gp] = uev
                gsb_free = s.emit("vector", lambda e, ic=ic: e.tensor_tensor(
                    big3[:, ic * TW:(ic + 1) * TW], gu_g[:], gu_u[:], ALU.mult),
                    waits=[("scalar", gev), ("vector", uev), ("tensor", fuse_user)]
                    + (ctx_done_all + [("spx", spx_cnt)]
                       if w == 0 else []), inc=1)
                gate_par ^= 1
            fuse_done = gsb_free


            gdone = 0
            for oc in range(EC):
                dmm = 0
                dp = dpar
                for icq in range(IC // SL):
                    sp2 = (oc * (IC // SL) + icq) % 2
                    wdp[sp2] = s.emit(
                        SG, lambda e, oc=oc, icq=icq, sp2=sp2: e.dma_start(
                            out=wd_slab(sp2), in_=w_down[oc, icq]),
                        waits=[("tensor", wdp_free[sp2])],
                        inc=16, sem=f"wd{sp2}")
                    for ick in range(SL):
                        ic = icq * SL + ick
                        wt = []
                        if ick == 0:
                            wt = [(f"wd{sp2}", wdp[sp2]), ("vector", fuse_done)]
                            if icq == 0:
                                wt += bankw(4 + dp)
                        dmm = s.emit("tensor", lambda e, ic=ic, ick=ick, sp2=sp2, dp=dp: e.matmul(
                            A[4 + dp][:, 0:TW], wd_slab(sp2)[:, ick * 128:(ick + 1) * 128],
                            big3[:, ic * TW:(ic + 1) * TW],
                            start=(ic == 0), stop=(ic == IC - 1)),
                            waits=wt, inc=(1 if ick == SL - 1 else 0))
                    wdp_free[sp2] = dmm
                fuse_user = dmm
                banks[4 + dp] = s.emit("scalar", lambda e, oc=oc, dp=dp: e.activation(
                    gT2[:, oc * TW:(oc + 1) * TW], A[4 + dp][:, 0:TW], AF.Silu,
                    scale=sd_sb[:, oc:oc + 1]),
                    waits=[("tensor", dmm)]
                    + ([("vector", gT_user)] if oc == 0 else []), inc=1)
                gdone = banks[4 + dp]
                dpar ^= 1

            def emit_fin(eoc, rp, rdone, last):
                nonlocal tp_par
                for i in range(NWT):
                    p = tp_par
                    tp = s.emit("tensor", lambda e, i=i, p=p, rp=rp: e.transpose(
                        TP[p][:, 0:128], r_ap(rp)[:, i * 128:(i + 1) * 128], id_bf[:]),
                        waits=[("vector", rdone)] + tpw(p), inc=1)
                    r_free[rp] = tp
                    fin = s.emit("vector", lambda e, i=i, eoc=eoc, p=p: e.tensor_tensor(
                        xt_ap(i)[:, eoc * 128:(eoc + 1) * 128], TP[p][:, 0:128],
                        xt_ap(i)[:, eoc * 128:(eoc + 1) * 128], ALU.add),
                        waits=[("tensor", tp), (f"ldx{i}", ld_x[i])], inc=1)
                    tpv[p] = fin
                    xt_user_v[i] = fin
                    tp_par ^= 1
                    if last:
                        st_out[i] = s.emit(
                            SY, lambda e, w=w, i=i: e.dma_start(
                                out=out[(w * NWT + i) * 128:(w * NWT + i + 1) * 128, :],
                                in_=xt_ap(i)),
                            waits=[("vector", fin)], inc=16, sem=f"sout{i}")

            if w + 1 < NWIN:
                emit_gu(0)
                emit_gu(1)

            pend = []
            for eoc in range(EC):
                sp = eoc % 3
                wpp[sp] = s.emit(
                    SG, lambda e, eoc=eoc, sp=sp: e.dma_start(
                        out=wqkv_slab(sp), in_=WpT[eoc]),
                    waits=[("tensor", wpp_free[sp])],
                    inc=16, sem=f"w1{sp}")
                dp = dpar
                smm = 0
                for epc in range(EC):
                    wt = []
                    if epc == 0:
                        wt = [(f"w1{sp}", wpp[sp]), ("ldc", ldc)] + bankw(4 + dp)
                    smm = s.emit("tensor", lambda e, epc=epc, sp=sp, dp=dp: e.matmul(
                        A[4 + dp][:, 0:TW], wqkv_slab(sp)[:, epc * 128:(epc + 1) * 128],
                        ctxT2[:, epc * TW:(epc + 1) * TW],
                        start=(epc == 0), stop=(epc == EC - 1)),
                        waits=wt, inc=(1 if epc == EC - 1 else 0))
                wpp_free[sp] = smm
                ctxT_user = smm
                rpar = eoc % 3
                rdone = s.emit("vector", lambda e, eoc=eoc, dp=dp, rp=rpar: e.scalar_tensor_tensor(
                    r_ap(rp), A[4 + dp][:, 0:TW], bp_sb[:, eoc:eoc + 1],
                    gT2[:, eoc * TW:(eoc + 1) * TW], ALU.add, ALU.mult),
                    waits=[("tensor", smm), ("scalar", gdone), ("tensor", r_free[rpar])],
                    inc=1)
                gT_user = rdone
                bankv[4 + dp] = rdone
                dpar ^= 1
                pend.append((eoc, rpar, rdone))
                if len(pend) > 2:
                    pe_ = pend.pop(0)
                    emit_fin(pe_[0], pe_[1], pe_[2], False)
            for j, pe_ in enumerate(pend):
                emit_fin(pe_[0], pe_[1], pe_[2], j == len(pend) - 1)

        fw = [(f"sout{i}", st_out[i]) for i in range(NWT)]
        s.emit(SY, lambda e: e.nop(), waits=fw)

        s.replay(block)
    return nc


def _prep_inputs(inputs):
    E = np.asarray(inputs["L0_attn_Wq"]).shape[0]
    I = np.asarray(inputs["L0_ffn_w_gate"]).shape[1]
    EC, IC = E // 128, I // 128
    SL = min(16, IC)
    c = np.ascontiguousarray

    def tile_sq(W):
        return c(np.asarray(W, np.float32).T.reshape(EC, 128, EC, 128)
                 .transpose(2, 1, 0, 3).reshape(EC, 128, E))

    def tile_gu(W):
        return c(np.asarray(W).reshape(EC, 128, IC, 128)
                 .transpose(2, 1, 0, 3).reshape(IC, 128, E))

    wd = np.asarray(inputs["L0_ffn_w_down"])
    wd_t = c(wd.reshape(IC // SL, SL, 128, EC, 128)
             .transpose(3, 0, 2, 1, 4).reshape(EC, IC // SL, 128, SL * 128))
    return {
        "WqT": tile_sq(inputs["L0_attn_Wq"]),
        "WkT": tile_sq(inputs["L0_attn_Wk"]),
        "WvT": tile_sq(inputs["L0_attn_Wv"]),
        "WpT": tile_sq(inputs["L0_attn_Wp"]),
        "bp": c(np.asarray(inputs["L0_attn_bp"]).reshape(EC, 128).T),
        "w_gate": tile_gu(inputs["L0_ffn_w_gate"]),
        "w_up": tile_gu(inputs["L0_ffn_w_up"]),
        "w_down": wd_t,
        "s_gate": c(np.asarray(inputs["L0_ffn_s_gate"]).reshape(IC, 128).T),
        "s_up": c(np.asarray(inputs["L0_ffn_s_up"]).reshape(IC, 128).T),
        "s_down": c(np.asarray(inputs["L0_ffn_s_down"]).reshape(EC, 128).T),
        "ident": np.eye(128, dtype=np.float32),
        "masku": np.triu(np.ones((128, 128), dtype=np.float32)),
    }


def kernel(L0_attn_x, L0_attn_Wq, L0_attn_Wk, L0_attn_Wv, L0_attn_Wp, L0_attn_bp,
           L0_ffn_w_gate, L0_ffn_w_up, L0_ffn_w_down, L0_ffn_s_gate, L0_ffn_s_up,
           L0_ffn_s_down):
    xfull = np.ascontiguousarray(np.asarray(L0_attn_x, dtype=np.float32))
    B, T, E = xfull.shape
    I = np.asarray(L0_ffn_w_gate).shape[1]
    common = _prep_inputs(dict(
        L0_attn_Wq=L0_attn_Wq, L0_attn_Wk=L0_attn_Wk, L0_attn_Wv=L0_attn_Wv,
        L0_attn_Wp=L0_attn_Wp, L0_attn_bp=L0_attn_bp,
        L0_ffn_w_gate=L0_ffn_w_gate, L0_ffn_w_up=L0_ffn_w_up,
        L0_ffn_w_down=L0_ffn_w_down, L0_ffn_s_gate=L0_ffn_s_gate,
        L0_ffn_s_up=L0_ffn_s_up, L0_ffn_s_down=L0_ffn_s_down))
    nc = build(T=T, E=E, I=I, H=16)
    in_maps = [dict(common, x=xfull[b]) for b in range(B)]
    res = run_bass_kernel_spmd(nc, in_maps, core_ids=list(range(B)))
    return np.stack([np.asarray(r["out"]) for r in res.results], axis=0)


# version 15
# speedup vs baseline: 1.1191x; 1.1191x over previous
import contextlib
import math
import numpy as np
import concourse.bass as bass
import concourse.mybir as mybir
from concourse.bass_utils import run_bass_kernel_spmd

F32 = mybir.dt.float32
BF16 = mybir.dt.bfloat16
I8 = mybir.dt.int8
AF = mybir.ActivationFunctionType
ALU = mybir.AluOpType
AX = mybir.AxisListType

ENGS = ("sync", "gpsimd", "tensor", "vector", "scalar")

ALPHA = 1.0 / math.sqrt(128.0)
M0 = 12.0


class Sched:

    def __init__(self, nc, ctx):
        self.nc = nc
        self.ctx = ctx
        self.ops = {e: [] for e in ENGS}
        self.count = {}
        self.sems = {}
        for e in ("tensor", "vector", "scalar"):
            self._mk(e)

    def _mk(self, name):
        self.sems[name] = self.ctx.enter_context(self.nc.semaphore("s_" + name))
        self.count[name] = 0
        return self.sems[name]

    def emit(self, eng, builder, waits=(), inc=0, sem=None):
        if inc and sem is None:
            sem = eng
        if sem is not None and sem not in self.sems:
            self._mk(sem)
        waits = [(sn, v) for (sn, v) in waits if v > 0]
        n = 0
        if inc:
            self.count[sem] += inc
            n = self.count[sem]
        sched = self

        def op(e):
            for sn, v in waits:
                e.wait_ge(sched.sems[sn], v)
            inst = builder(e)
            if inc:
                inst.then_inc(sched.sems[sem], inc)
        self.ops[eng].append(op)
        return n

    def snap(self):
        return dict(self.count)

    def replay(self, block):
        for eng in ENGS:
            def mk(eng):
                def run(e):
                    for op in self.ops[eng]:
                        op(e)
                return run
            getattr(block, eng)(mk(eng))


def build(T=4096, E=2048, I=8192, H=16):
    D = 128
    EC = E // 128
    IC = I // 128
    THEAD = T // H
    VQ = THEAD * EC
    NQT = VQ // 128
    QW = min(512, VQ)
    NQUART = VQ // QW
    NACC = QW // 128
    G = 2
    NGRP = H // G
    GT = G * THEAD
    TW = GT
    NWT = TW // 128
    NWIN = T // TW
    NTILE = T // 128
    NCH = max(1, E // 512)
    SL = min(16, IC)
    assert NACC <= 4 and GT == TW and EC == H

    nc = bass.Bass("TRN2", target_bir_lowering=False)
    ap = lambda *a, **k: nc.dram_tensor(*a, **k).ap()
    x = ap("x", [T, E], F32, kind="ExternalInput")
    WqT = ap("WqT", [EC, 128, E], F32, kind="ExternalInput")
    WkT = ap("WkT", [EC, 128, E], F32, kind="ExternalInput")
    WvT = ap("WvT", [EC, 128, E], F32, kind="ExternalInput")
    WpT = ap("WpT", [EC, 128, E], F32, kind="ExternalInput")
    bp = ap("bp", [128, EC], F32, kind="ExternalInput")
    w_gate = ap("w_gate", [IC, 128, E], I8, kind="ExternalInput")
    w_up = ap("w_up", [IC, 128, E], I8, kind="ExternalInput")
    w_down = ap("w_down", [EC, IC // SL, 128, SL * 128], I8, kind="ExternalInput")
    s_gate = ap("s_gate", [128, IC], F32, kind="ExternalInput")
    s_up = ap("s_up", [128, IC], F32, kind="ExternalInput")
    s_down = ap("s_down", [128, EC], F32, kind="ExternalInput")
    ident = ap("ident", [128, 128], F32, kind="ExternalInput")
    masku = ap("masku", [128, 128], F32, kind="ExternalInput")
    out = ap("out", [T, E], F32, kind="ExternalOutput")

    xnT_s = ap("xnT_s", [EC, 128, T], BF16, kind="Internal")
    ctx_s = ap("ctx_s", [H, 128, VQ], BF16, kind="Internal")

    with contextlib.ExitStack() as ctx:
        sb = lambda name, shape, dt: ctx.enter_context(nc.sbuf_tensor(name, shape, dt))
        psumt = lambda name, shape, dt: ctx.enter_context(nc.psum_tensor(name, shape, dt))

        o_vext = 0
        o_vhT = o_vext + NQT * (D + 1)
        o_ctx = o_vhT + 2 * VQ
        o_pT = o_ctx + 2 * VQ
        o_xn0 = o_pT + 2 * QW
        o_xnt = o_xn0 + E
        big3_cols = max(IC * TW, o_xnt + 4 * E)

        id_bf = sb("id_bf", [128, 128], BF16)
        mask_bf = sb("mask_bf", [128, 128], BF16)
        sg_sb = sb("sg_sb", [128, IC], F32)
        su_sb = sb("su_sb", [128, IC], F32)
        sd_sb = sb("sd_sb", [128, EC], F32)
        bp_sb = sb("bp_sb", [128, EC], F32)
        st6 = sb("st6", [128, 6 * NCH], F32)
        mv = sb("mv", [128, 2], F32)
        sdv = sb("sdv", [128, 2], F32)
        rcp = sb("rcp", [128, 2], F32)
        gu_g = sb("gu_g", [128, TW], BF16)
        gu_u = sb("gu_u", [128, TW], BF16)
        r_eoc = sb("r_eoc", [128, 3 * TW], BF16)

        big1 = sb("big1", [128, 2 * EC * GT], BF16)
        big2 = sb("big2", [128, max(2 * H * TW, 4 * VQ)], BF16)
        big3 = sb("big3", [128, big3_cols], BF16)
        big4 = sb("big4", [128, 3 * EC * 128], BF16)
        wgu = sb("wgu", [128, 4 * EC * 128], BF16)
        wds = sb("wds", [128, 3 * SL * 128], BF16)
        xt = sb("xt", [128, NWT * E], F32)

        def xnT_buf(par):
            return big1[:, par * EC * GT:(par + 1) * EC * GT]

        def qh_ap(hh):
            return big2[:, hh * VQ:(hh + 1) * VQ]

        def kh_ap(hh):
            return big2[:, (2 + hh) * VQ:(3 + hh) * VQ]

        vext = big3[:, o_vext:o_vhT]

        def vhT_ap(hh):
            return big3[:, o_vhT + hh * VQ: o_vhT + (hh + 1) * VQ]

        def ctx_sb(par):
            return big3[:, o_ctx + par * VQ: o_ctx + (par + 1) * VQ]

        def pT_ap(par):
            return big3[:, o_pT + par * QW: o_pT + (par + 1) * QW]

        xn0 = big3[:, o_xn0:o_xn0 + E]

        def xnt_dst(a, ec):
            off = o_xnt + ec * 4 * 128 + a * 128
            return big3[:, off:off + 128]

        xnt_grp = big3[:, o_xnt:o_xnt + 4 * E]

        raw2 = big2[:, 0:H * TW]
        gT2 = big2[:, 0:EC * TW]
        ctxT2 = big2[:, H * TW:2 * H * TW]

        def wqkv_slab(par):
            return big4[:, par * EC * 128:(par + 1) * EC * 128]

        def wg_slab(par):
            return wgu[:, par * EC * 128:(par + 1) * EC * 128]

        def wu_slab(par):
            return wgu[:, (2 + par) * EC * 128:(3 + par) * EC * 128]

        def wd_slab(par):
            return wds[:, par * SL * 128:(par + 1) * SL * 128]

        def xt_ap(slot):
            return xt[:, slot * E:(slot + 1) * E]

        def r_ap(par):
            return r_eoc[:, par * TW:(par + 1) * TW]

        A = [psumt(f"A{i}", [128, 512], F32) for i in range(6)]
        TP = [psumt(f"TP{i}", [128, 1024], BF16) for i in range(2)]
        bankv = [0] * 6
        banks = [0] * 6
        tpv = [0, 0]
        tps = [0, 0]

        s = Sched(nc, ctx)
        block = ctx.enter_context(nc.Block())
        SG = "gpsimd"
        SY = "sync"

        def bankw(b):
            return [("vector", bankv[b]), ("scalar", banks[b])]

        def tpw(p):
            return [("vector", tpv[p]), ("scalar", tps[p])]

        for dst, src in ((id_bf, ident), (mask_bf, masku), (sg_sb, s_gate),
                         (su_sb, s_up), (sd_sb, s_down), (bp_sb, bp)):
            cst = s.emit(SG, lambda e, dst=dst, src=src: e.dma_start(out=dst[:], in_=src),
                         inc=16, sem="c")
        CST = cst

        xt_user_v = [0] * NWT
        xt_user_s = [0] * NWT
        ld_x = [0] * NWT
        tp_par = 0
        spx_cnt = 0
        xn0_free_t = 0
        prev_bna = 0
        prev_std = 0
        prev_xnd = 0

        for i in range(NTILE):
            slot = i % NWT
            ld_x[slot] = s.emit(
                SY, lambda e, i=i, slot=slot: e.dma_start(
                    out=xt_ap(slot), in_=x[i * 128:(i + 1) * 128, :]),
                waits=[("vector", xt_user_v[slot]), ("scalar", xt_user_s[slot])],
                inc=16, sem=f"ldx{slot}")
            w_ld = [(f"ldx{slot}", ld_x[slot])]
            bns = 0
            cw = E // NCH
            for c in range(NCH):
                bns = s.emit("vector", lambda e, c=c, cw=cw, slot=slot: e.bn_stats(
                    st6[:, 6 * c:6 * c + 6], xt_ap(slot)[:, cw * c:cw * (c + 1)]),
                    waits=w_ld + ([("vector", prev_bna)] if c == 0 else []), inc=1)
            bna = s.emit("vector", lambda e: e.bn_aggr(mv[:], st6[:]),
                         waits=[("vector", bns), ("scalar", prev_std),
                                ("vector", prev_xnd)], inc=1)
            prev_bna = bna
            std = s.emit("scalar", lambda e: e.activation(
                sdv[:, 0:1], mv[:, 1:2], AF.Sqrt, bias=np.float32(1e-5)),
                waits=[("vector", bna)], inc=1)
            prev_std = std
            rst = s.emit("vector", lambda e: e.reciprocal(sdv[:, 1:2], sdv[:, 0:1]),
                         waits=[("scalar", std)], inc=1)
            xnd_a = s.emit("vector", lambda e, slot=slot: e.tensor_scalar(
                xn0[:, 0:E // 2], xt_ap(slot)[:, 0:E // 2], mv[:, 0:1],
                sdv[:, 1:2], ALU.subtract, ALU.mult),
                waits=[("vector", rst), ("tensor", xn0_free_t)], inc=1)
            xnd = s.emit("vector", lambda e, slot=slot: e.tensor_scalar(
                xn0[:, E // 2:E], xt_ap(slot)[:, E // 2:E], mv[:, 0:1],
                sdv[:, 1:2], ALU.subtract, ALU.mult),
                waits=[("vector", xnd_a)], inc=1)
            prev_xnd = xnd
            xt_user_v[slot] = xnd
            xt_user_s[slot] = std
            a = i % 4
            last_evac = 0
            for ec in range(EC):
                p = tp_par
                tp = s.emit("tensor", lambda e, ec=ec, p=p: e.transpose(
                    TP[p][:, 0:128], xn0[:, ec * 128:(ec + 1) * 128], id_bf[:]),
                    waits=[("vector", xnd_a if ec < EC // 2 else xnd), ("c", CST)]
                    + tpw(p), inc=1)
                if ec == EC - 1:
                    xn0_free_t = tp
                wsp = [("spx", 16 * (i // 4))] if (a == 0 and ec == 0) else []
                tps[p] = s.emit("scalar", lambda e, ec=ec, p=p, a=a: e.copy(
                    xnt_dst(a, ec), TP[p][:, 0:128]),
                    waits=[("tensor", tp)] + wsp, inc=1)
                last_evac = tps[p]
                tp_par ^= 1
            if a == 3:
                spx_cnt = s.emit(
                    SY, lambda e, i=i: e.dma_start(
                        out=xnT_s[:, :, (i - 3) * 128:(i + 1) * 128].rearrange(
                            "ec p t -> p ec t"),
                        in_=xnt_grp.rearrange("p (ec t) -> p ec t", t=4 * 128)),
                    waits=[("scalar", last_evac)], inc=16, sem="spx")

        def grp_ready(g):
            return [("spx", 16 * (g + 1))]

        big1_free = [0, 0]
        ldg = [0, 0]
        w1 = [0, 0, 0]
        w1_free = [0, 0, 0]
        qkv_par = 0
        big2_free = 0
        vext_free = 0
        vhT_free = [0, 0]
        ctx_busy = [0, 0]
        ctx_done_all = []
        pT_free = [0, 0]
        head_par = 0
        slab_par = 0
        qkv_ev = [0, 0]

        for g in range(NGRP):
            gpar = g % 2
            ldg[gpar] = s.emit(
                SY, lambda e, g=g, gp=gpar: e.dma_start(
                    out=xnT_buf(gp).rearrange("p (ec t) -> p ec t", ec=EC),
                    in_=xnT_s[:, :, g * GT:(g + 1) * GT].rearrange("ec p t -> p ec t")),
                waits=grp_ready(g) + [("tensor", big1_free[gpar])],
                inc=16, sem=f"ldg{gpar}")

            b2f = big2_free
            mm = 0
            vdone = None
            for WT, proj in ((WvT, "v"), (WqT, "q"), (WkT, "k")):
                for cp in range(EC):
                    sp = slab_par
                    w1[sp] = s.emit(
                        SG, lambda e, WT=WT, cp=cp, sp=sp: e.dma_start(
                            out=wqkv_slab(sp), in_=WT[cp]),
                        waits=[("tensor", w1_free[sp])],
                        inc=16, sem=f"w1{sp}")
                    qp = qkv_par
                    for ec in range(EC):
                        wt = []
                        if ec == 0:
                            wt = [(f"w1{sp}", w1[sp]), (f"ldg{gpar}", ldg[gpar])] + bankw(qp)
                        mm = s.emit("tensor", lambda e, ec=ec, sp=sp, qp=qp, gp=gpar: e.matmul(
                            A[qp][:, 0:GT], wqkv_slab(sp)[:, ec * 128:(ec + 1) * 128],
                            xnT_buf(gp)[:, ec * GT:(ec + 1) * GT],
                            start=(ec == 0), stop=(ec == EC - 1)),
                            waits=wt, inc=(1 if ec == EC - 1 else 0))
                    w1_free[sp] = mm
                    for hh in range(G):
                        if proj == "q":
                            dst = qh_ap(hh)
                        elif proj == "k":
                            dst = kh_ap(hh)
                        else:
                            dst = vhT_ap(hh)
                        evw = [("tensor", mm), ("tensor", b2f)]
                        if proj == "v":
                            evw.append(("tensor", vhT_free[hh]))
                        bankv[qp] = s.emit("vector", lambda e, dst=dst, cp=cp, qp=qp, hh=hh: e.tensor_copy(
                            dst[:, cp::EC], A[qp][:, hh * THEAD:(hh + 1) * THEAD]),
                            waits=evw, inc=1)
                        qkv_ev[hh] = bankv[qp]
                    qkv_par = (qkv_par + 1) % 4
                    slab_par = (slab_par + 1) % 3
                if proj == "v":
                    vdone = [("vector", qkv_ev[0]), ("vector", qkv_ev[1])]
            big1_free[gpar] = mm
            qkv_all = [("vector", qkv_ev[0]), ("vector", qkv_ev[1])]

            for hh in range(G):
                h = g * G + hh
                hpar = head_par
                on = s.emit("vector", lambda e: e.memset(vext[:, D::(D + 1)], 1.0),
                            waits=[("tensor", vext_free)], inc=1)
                last_vx = 0
                for kt in range(NQT):
                    p = tp_par
                    tp = s.emit("tensor", lambda e, kt=kt, p=p, hh=hh: e.transpose(
                        TP[p][:, 0:128], vhT_ap(hh)[:, kt * 128:(kt + 1) * 128], id_bf[:]),
                        waits=(vdone if kt <= 1 else []) + tpw(p)
                        + ([("tensor", vext_free)] if kt <= 1 else []), inc=1)
                    if kt == NQT - 1:
                        vhT_free[hh] = tp
                    tps[p] = s.emit("scalar", lambda e, kt=kt, p=p: e.copy(
                        vext[:, kt * (D + 1):kt * (D + 1) + D], TP[p][:, 0:128]),
                        waits=[("tensor", tp), ("vector", on)], inc=1)
                    last_vx = tps[p]
                    tp_par ^= 1

                last_ctx_mm = 0
                for Qq in range(NQUART):
                    qbase = Qq * QW
                    ktmax = (qbase + QW) // 128
                    sinfo = {}

                    def emit_S(kt, Qq=Qq, qbase=qbase, hh=hh):
                        qlo = max(qbase, kt * 128)
                        width = qbase + QW - qlo
                        sp = kt % 2
                        pp = kt % 2
                        smm = s.emit("tensor", lambda e, kt=kt, qlo=qlo, width=width, sp=sp, hh=hh: e.matmul(
                            A[4 + sp][:, 0:width], kh_ap(hh)[:, kt * 128:(kt + 1) * 128],
                            qh_ap(hh)[:, qlo:qlo + width], start=True, stop=True),
                            waits=qkv_all + bankw(4 + sp) if kt <= 1 else bankw(4 + sp),
                            inc=1)
                        hi_n = 1
                        mid = width - hi_n * 128
                        exp_hi = s.emit("scalar", lambda e, width=width, mid=mid, sp=sp, pp=pp: e.activation(
                            pT_ap(pp)[:, mid:width], A[4 + sp][:, mid:width], AF.Exp,
                            bias=np.float32(-M0), scale=np.float32(ALPHA)),
                            waits=[("tensor", smm), ("tensor", pT_free[pp])],
                            inc=1)
                        exp_lo = exp_hi
                        if mid > 0:
                            exp_lo = s.emit("scalar", lambda e, mid=mid, sp=sp, pp=pp: e.activation(
                                pT_ap(pp)[:, 0:mid], A[4 + sp][:, 0:mid], AF.Exp,
                                bias=np.float32(-M0), scale=np.float32(ALPHA)),
                                inc=1)
                        banks[4 + sp] = exp_lo
                        mskd = 0
                        if qlo == kt * 128:
                            mskd = s.emit("vector", lambda e, pp=pp: e.tensor_tensor(
                                pT_ap(pp)[:, 0:128], pT_ap(pp)[:, 0:128], mask_bf[:], ALU.mult),
                                waits=[("scalar", exp_lo), ("c", CST)], inc=1)
                        sinfo[kt] = (exp_hi, exp_lo, mskd, qlo, hi_n)

                    emit_S(0)
                    for kt in range(ktmax):
                        if kt + 1 < ktmax:
                            emit_S(kt + 1)
                        exp_hi, exp_lo, mskd, qlo, hi_n = sinfo.pop(kt)
                        pp = kt % 2
                        first_qt = max(kt, qbase // 128)
                        qts = list(range(first_qt, qbase // 128 + NACC))[::-1]
                        for qi_, qt in enumerate(qts):
                            j = qt - qbase // 128
                            wt = []
                            if qi_ == 0:
                                wt = [("scalar", exp_hi), ("scalar", last_vx)]
                            if qi_ == hi_n:
                                wt += [("scalar", exp_lo)]
                            if mskd and qi_ == len(qts) - 1:
                                wt.append(("vector", mskd))
                            if kt == 0:
                                wt += bankw(j)
                            cmm = s.emit("tensor", lambda e, qt=qt, kt=kt, j=j, pp=pp, qlo=qlo: e.matmul(
                                A[j][:, 0:D + 1], pT_ap(pp)[:, qt * 128 - qlo:qt * 128 - qlo + 128],
                                vext[:, kt * (D + 1):(kt + 1) * (D + 1)],
                                start=(kt == 0), stop=(kt == qt)),
                                waits=wt, inc=(1 if qt == qts[-1] else 0))
                            if qt == qts[-1]:
                                last_ctx_mm = cmm
                        pT_free[pp] = last_ctx_mm
                        if kt >= qbase // 128:
                            j = kt - qbase // 128
                            qt = kt
                            jp = j % 2
                            rc = s.emit("vector", lambda e, j=j, jp=jp: e.reciprocal(
                                rcp[:, jp:jp + 1], A[j][:, D:D + 1]),
                                waits=[("tensor", last_ctx_mm)], inc=1)
                            bankv[j] = s.emit("vector", lambda e, j=j, qt=qt, hp=hpar, jp=jp: e.tensor_scalar(
                                ctx_sb(hp)[:, qt * 128:(qt + 1) * 128], A[j][:, 0:D],
                                rcp[:, jp:jp + 1], None, ALU.mult),
                                waits=[("vector", rc), (f"sctx{hpar}", ctx_busy[hpar])],
                                inc=1)
                vext_free = last_ctx_mm
                big2_free = last_ctx_mm
                ctx_busy[hpar] = s.emit(
                    SY, lambda e, h=h, hp=hpar: e.dma_start(
                        out=ctx_s[h], in_=ctx_sb(hp)),
                    waits=[("vector", bankv[NACC - 1])], inc=16, sem=f"sctx{hpar}")
                ctx_done_all = [("sctx0", ctx_busy[0]), ("sctx1", ctx_busy[1])]
                head_par ^= 1

        F = s.snap()
        fence = [("tensor", F["tensor"]), ("vector", F["vector"]), ("scalar", F["scalar"])]

        ld2 = [0, 0]
        raw_free = 0
        gT_user = 0
        ctxT_user = 0
        wgp = [0, 0]
        wgp_free = [0, 0]
        wup = [0, 0]
        wup_free = [0, 0]
        wdp = [0, 0, 0]
        wdp_free = [0, 0, 0]
        wpp = list(w1)
        wpp_free = list(w1_free)
        st_out = [0] * NWT
        gate_par = 0
        dpar = 0
        fuse_user = 0
        gsb_free = 0
        r_free = [0, 0, 0]
        first2 = True

        for w in range(NWIN):
            wpar = w % 2
            wt0 = list(fence) if first2 else []
            ld2[wpar] = s.emit(
                SY, lambda e, w=w, wp=wpar: e.dma_start(
                    out=xnT_buf(wp).rearrange("p (ec t) -> p ec t", ec=EC),
                    in_=xnT_s[:, :, w * TW:(w + 1) * TW].rearrange("ec p t -> p ec t")),
                waits=wt0 + grp_ready(w) + [("tensor", big1_free[wpar])],
                inc=16, sem=f"ldg{wpar}")
            ldc = s.emit(
                SY, lambda e, w=w: e.dma_start(
                    out=raw2.rearrange("p (h t) -> p h t", h=H),
                    in_=ctx_s[:, :, w * TW:(w + 1) * TW].rearrange("h p t -> p h t")),
                waits=wt0 + ctx_done_all + [("vector", gT_user), ("tensor", raw_free)],
                inc=16, sem="ldc")
            for i in range(NWT):
                ld_x[i] = s.emit(
                    SY, lambda e, w=w, i=i: e.dma_start(
                        out=xt_ap(i), in_=x[(w * NWT + i) * 128:(w * NWT + i + 1) * 128, :]),
                    waits=wt0 + [(f"sout{i}", st_out[i]),
                                 ("vector", xt_user_v[i]), ("scalar", xt_user_s[i])],
                    inc=16, sem=f"ldx{i}")
                xt_user_v[i] = 0
                xt_user_s[i] = 0
            first2 = False

            def emit_gu(ic):
                sp = ic % 2
                wgp[sp] = s.emit(
                    SG, lambda e, ic=ic, sp=sp: e.dma_start(
                        out=wg_slab(sp), in_=w_gate[ic]),
                    waits=[("tensor", wgp_free[sp])],
                    inc=16, sem=f"wg{sp}")
                wup[sp] = s.emit(
                    SG, lambda e, ic=ic, sp=sp: e.dma_start(
                        out=wu_slab(sp), in_=w_up[ic]),
                    waits=[("tensor", wup_free[sp])],
                    inc=16, sem=f"wu{sp}")

            for ic in range(IC):
                sp = ic % 2
                if w == 0 or ic >= 2:
                    emit_gu(ic)
                gp = gate_par
                mmg = 0
                for ec in range(EC):
                    wt = []
                    if ec == 0:
                        wt = [(f"wg{sp}", wgp[sp]), (f"ldg{wpar}", ld2[wpar])] + bankw(gp)
                    mmg = s.emit("tensor", lambda e, ec=ec, sp=sp, gp=gp, wp=wpar: e.matmul(
                        A[gp][:, 0:TW], wg_slab(sp)[:, ec * 128:(ec + 1) * 128],
                        xnT_buf(wp)[:, ec * TW:(ec + 1) * TW],
                        start=(ec == 0), stop=(ec == EC - 1)),
                        waits=wt, inc=(1 if ec == EC - 1 else 0))
                mmu = 0
                for ec in range(EC):
                    wt = []
                    if ec == 0:
                        wt = [(f"wu{sp}", wup[sp])] + bankw(2 + gp)
                    mmu = s.emit("tensor", lambda e, ec=ec, sp=sp, gp=gp, wp=wpar: e.matmul(
                        A[2 + gp][:, 0:TW], wu_slab(sp)[:, ec * 128:(ec + 1) * 128],
                        xnT_buf(wp)[:, ec * TW:(ec + 1) * TW],
                        start=(ec == 0), stop=(ec == EC - 1)),
                        waits=wt, inc=(1 if ec == EC - 1 else 0))
                wgp_free[sp] = mmg
                wup_free[sp] = mmu
                big1_free[wpar] = mmu
                gev = s.emit("scalar", lambda e, ic=ic, gp=gp: e.activation(
                    gu_g[:], A[gp][:, 0:TW], AF.Silu, scale=sg_sb[:, ic:ic + 1]),
                    waits=[("tensor", mmg), ("vector", gsb_free)], inc=1)
                banks[gp] = gev
                uev = s.emit("vector", lambda e, ic=ic, gp=gp: e.tensor_scalar(
                    gu_u[:], A[2 + gp][:, 0:TW], su_sb[:, ic:ic + 1], None, ALU.mult),
                    waits=[("tensor", mmu), ("vector", gsb_free)], inc=1)
                bankv[2 + gp] = uev
                gsb_free = s.emit("vector", lambda e, ic=ic: e.tensor_tensor(
                    big3[:, ic * TW:(ic + 1) * TW], gu_g[:], gu_u[:], ALU.mult),
                    waits=[("scalar", gev), ("vector", uev), ("tensor", fuse_user)]
                    + (ctx_done_all + [("spx", spx_cnt)]
                       if w == 0 else []), inc=1)
                gate_par ^= 1
            fuse_done = gsb_free

            def emit_ctp():
                nonlocal tp_par, raw_free, last_ctp_s
                for h in range(H):
                    for i in range(NWT):
                        p = tp_par
                        tp = s.emit("tensor", lambda e, h=h, i=i, p=p: e.transpose(
                            TP[p][:, 0:128], raw2[:, h * TW + i * 128: h * TW + (i + 1) * 128],
                            id_bf[:]),
                            waits=[("ldc", ldc), ("vector", gT_user)] + tpw(p), inc=1)
                        raw_free = tp
                        tps[p] = s.emit("scalar", lambda e, h=h, i=i, p=p: e.copy(
                            ctxT2[:, h * TW + i * 128: h * TW + (i + 1) * 128], TP[p][:, 0:128]),
                            waits=[("tensor", tp), ("tensor", ctxT_user)], inc=1)
                        last_ctp_s = tps[p]
                        tp_par ^= 1

            last_ctp_s = 0

            gdone = 0
            for oc in range(EC):
                dmm = 0
                dp = dpar
                for icq in range(IC // SL):
                    sp2 = (oc * (IC // SL) + icq) % 3
                    wdp[sp2] = s.emit(
                        SG, lambda e, oc=oc, icq=icq, sp2=sp2: e.dma_start(
                            out=wd_slab(sp2), in_=w_down[oc, icq]),
                        waits=[("tensor", wdp_free[sp2])],
                        inc=16, sem=f"wd{sp2}")
                    for ick in range(SL):
                        ic = icq * SL + ick
                        wt = []
                        if ick == 0:
                            wt = [(f"wd{sp2}", wdp[sp2]), ("vector", fuse_done)]
                            if icq == 0:
                                wt += bankw(4 + dp)
                        dmm = s.emit("tensor", lambda e, ic=ic, ick=ick, sp2=sp2, dp=dp: e.matmul(
                            A[4 + dp][:, 0:TW], wd_slab(sp2)[:, ick * 128:(ick + 1) * 128],
                            big3[:, ic * TW:(ic + 1) * TW],
                            start=(ic == 0), stop=(ic == IC - 1)),
                            waits=wt, inc=(1 if ick == SL - 1 else 0))
                    wdp_free[sp2] = dmm
                fuse_user = dmm
                if oc == 0:
                    emit_ctp()
                banks[4 + dp] = s.emit("scalar", lambda e, oc=oc, dp=dp: e.activation(
                    gT2[:, oc * TW:(oc + 1) * TW], A[4 + dp][:, 0:TW], AF.Silu,
                    scale=sd_sb[:, oc:oc + 1]),
                    waits=[("tensor", dmm), ("tensor", raw_free)], inc=1)
                gdone = banks[4 + dp]
                dpar ^= 1

            def emit_fin(eoc, rp, rdone, last):
                nonlocal tp_par
                for i in range(NWT):
                    p = tp_par
                    tp = s.emit("tensor", lambda e, i=i, p=p, rp=rp: e.transpose(
                        TP[p][:, 0:128], r_ap(rp)[:, i * 128:(i + 1) * 128], id_bf[:]),
                        waits=[("vector", rdone)] + tpw(p), inc=1)
                    r_free[rp] = tp
                    fin = s.emit("vector", lambda e, i=i, eoc=eoc, p=p: e.tensor_tensor(
                        xt_ap(i)[:, eoc * 128:(eoc + 1) * 128], TP[p][:, 0:128],
                        xt_ap(i)[:, eoc * 128:(eoc + 1) * 128], ALU.add),
                        waits=[("tensor", tp), (f"ldx{i}", ld_x[i])], inc=1)
                    tpv[p] = fin
                    xt_user_v[i] = fin
                    tp_par ^= 1
                    if last:
                        st_out[i] = s.emit(
                            SY, lambda e, w=w, i=i: e.dma_start(
                                out=out[(w * NWT + i) * 128:(w * NWT + i + 1) * 128, :],
                                in_=xt_ap(i)),
                            waits=[("vector", fin)], inc=16, sem=f"sout{i}")

            if w + 1 < NWIN:
                emit_gu(0)
                emit_gu(1)

            pend = []
            for eoc in range(EC):
                sp = eoc % 3
                wpp[sp] = s.emit(
                    SG, lambda e, eoc=eoc, sp=sp: e.dma_start(
                        out=wqkv_slab(sp), in_=WpT[eoc]),
                    waits=[("tensor", wpp_free[sp])],
                    inc=16, sem=f"w1{sp}")
                dp = dpar
                smm = 0
                for epc in range(EC):
                    wt = []
                    if epc == 0:
                        wt = [(f"w1{sp}", wpp[sp]), ("scalar", last_ctp_s)] + bankw(4 + dp)
                    smm = s.emit("tensor", lambda e, epc=epc, sp=sp, dp=dp: e.matmul(
                        A[4 + dp][:, 0:TW], wqkv_slab(sp)[:, epc * 128:(epc + 1) * 128],
                        ctxT2[:, epc * TW:(epc + 1) * TW],
                        start=(epc == 0), stop=(epc == EC - 1)),
                        waits=wt, inc=(1 if epc == EC - 1 else 0))
                wpp_free[sp] = smm
                ctxT_user = smm
                rpar = eoc % 3
                rdone = s.emit("vector", lambda e, eoc=eoc, dp=dp, rp=rpar: e.scalar_tensor_tensor(
                    r_ap(rp), A[4 + dp][:, 0:TW], bp_sb[:, eoc:eoc + 1],
                    gT2[:, eoc * TW:(eoc + 1) * TW], ALU.add, ALU.mult),
                    waits=[("tensor", smm), ("scalar", gdone), ("tensor", r_free[rpar])],
                    inc=1)
                gT_user = rdone
                bankv[4 + dp] = rdone
                dpar ^= 1
                pend.append((eoc, rpar, rdone))
                if len(pend) > 2:
                    pe_ = pend.pop(0)
                    emit_fin(pe_[0], pe_[1], pe_[2], False)
            for j, pe_ in enumerate(pend):
                emit_fin(pe_[0], pe_[1], pe_[2], j == len(pend) - 1)

        fw = [(f"sout{i}", st_out[i]) for i in range(NWT)]
        s.emit(SY, lambda e: e.nop(), waits=fw)

        s.replay(block)
    return nc


def _prep_inputs(inputs):
    E = np.asarray(inputs["L0_attn_Wq"]).shape[0]
    I = np.asarray(inputs["L0_ffn_w_gate"]).shape[1]
    EC, IC = E // 128, I // 128
    SL = min(16, IC)
    c = np.ascontiguousarray

    def tile_sq(W):
        return c(np.asarray(W, np.float32).T.reshape(EC, 128, EC, 128)
                 .transpose(2, 1, 0, 3).reshape(EC, 128, E))

    def tile_gu(W):
        return c(np.asarray(W).reshape(EC, 128, IC, 128)
                 .transpose(2, 1, 0, 3).reshape(IC, 128, E))

    wd = np.asarray(inputs["L0_ffn_w_down"])
    wd_t = c(wd.reshape(IC // SL, SL, 128, EC, 128)
             .transpose(3, 0, 2, 1, 4).reshape(EC, IC // SL, 128, SL * 128))
    return {
        "WqT": tile_sq(inputs["L0_attn_Wq"]),
        "WkT": tile_sq(inputs["L0_attn_Wk"]),
        "WvT": tile_sq(inputs["L0_attn_Wv"]),
        "WpT": tile_sq(inputs["L0_attn_Wp"]),
        "bp": c(np.asarray(inputs["L0_attn_bp"]).reshape(EC, 128).T),
        "w_gate": tile_gu(inputs["L0_ffn_w_gate"]),
        "w_up": tile_gu(inputs["L0_ffn_w_up"]),
        "w_down": wd_t,
        "s_gate": c(np.asarray(inputs["L0_ffn_s_gate"]).reshape(IC, 128).T),
        "s_up": c(np.asarray(inputs["L0_ffn_s_up"]).reshape(IC, 128).T),
        "s_down": c(np.asarray(inputs["L0_ffn_s_down"]).reshape(EC, 128).T),
        "ident": np.eye(128, dtype=np.float32),
        "masku": np.triu(np.ones((128, 128), dtype=np.float32)),
    }


def kernel(L0_attn_x, L0_attn_Wq, L0_attn_Wk, L0_attn_Wv, L0_attn_Wp, L0_attn_bp,
           L0_ffn_w_gate, L0_ffn_w_up, L0_ffn_w_down, L0_ffn_s_gate, L0_ffn_s_up,
           L0_ffn_s_down):
    xfull = np.ascontiguousarray(np.asarray(L0_attn_x, dtype=np.float32))
    B, T, E = xfull.shape
    I = np.asarray(L0_ffn_w_gate).shape[1]
    common = _prep_inputs(dict(
        L0_attn_Wq=L0_attn_Wq, L0_attn_Wk=L0_attn_Wk, L0_attn_Wv=L0_attn_Wv,
        L0_attn_Wp=L0_attn_Wp, L0_attn_bp=L0_attn_bp,
        L0_ffn_w_gate=L0_ffn_w_gate, L0_ffn_w_up=L0_ffn_w_up,
        L0_ffn_w_down=L0_ffn_w_down, L0_ffn_s_gate=L0_ffn_s_gate,
        L0_ffn_s_up=L0_ffn_s_up, L0_ffn_s_down=L0_ffn_s_down))
    nc = build(T=T, E=E, I=I, H=16)
    in_maps = [dict(common, x=xfull[b]) for b in range(B)]
    res = run_bass_kernel_spmd(nc, in_maps, core_ids=list(range(B)))
    return np.stack([np.asarray(r["out"]) for r in res.results], axis=0)
